# version 01
# baseline (speedup 1.0000x reference)
import math
import os
import numpy as np
import ml_dtypes
import concourse.bass as bass
import concourse.mybir as mybir
from concourse.bass_utils import run_bass_kernel_spmd
from contextlib import ExitStack

F32 = mybir.dt.float32
BF16 = mybir.dt.bfloat16
U8 = mybir.dt.uint8
AF = mybir.ActivationFunctionType
ALU = mybir.AluOpType
AX = mybir.AxisListType

D = 4096
S = 4096
B = 4
H = 32
G = 8
HD = 128
T = 512
NT = 4
NKT = 8
NDC = 32
NE = 3277
EPS = 1e-5
ISQ = 1.0 / math.sqrt(128.0)
CBND = math.log(1.05) - 0.5 * math.log(128.0)
NEG = -1.0e9

KCS = [12, 16, 28, 32]
TRI = [list(range(0, 4)) + list(range(8, 12)),
       list(range(4, 8)) + list(range(12, 16)),
       list(range(16, 20)) + list(range(24, 28)),
       list(range(20, 24)) + list(range(28, 32))]
Q0S = {0: [0, 512, 3072, 3584], 1: [1024, 1536, 2048, 2560]}

PHASES = int(os.environ.get('ATH_PHASES', '2'))
NKT_RUN = int(os.environ.get('ATH_KT', '8'))
P1PARTS = set(os.environ.get('ATH_P1', 'ln,tr,k,v').split(','))
P2PARTS = set(os.environ.get('ATH_P2', 'ln,q,att,wo,ffn,f3').split(','))
KPARTS = set(os.environ.get('ATH_KP', 'rope,norm,store').split(','))
ROPS = set(os.environ.get('ATH_ROPE', 'c1,c2,t2,t1m,add').split(','))
DEBUG_TAPS = ()


def _merge(deps):
    m = {}
    for d in deps:
        if d is None:
            continue
        s, v = d
        if m.get(s, -1) < v:
            m[s] = v
    return list(m.items())


class Buf:
    __slots__ = ("w", "rs")

    def __init__(self):
        self.w = []
        self.rs = []

    def rdeps(self):
        return list(self.w)

    def wdeps(self):
        return list(self.w) + list(self.rs)

    def note_read(self, ev):
        if ev is not None:
            self.rs.append(ev)
            if len(self.rs) > 8:
                self.rs = _merge(self.rs)

    def note_write(self, ev):
        self.w = [] if ev is None else _merge(self.w + [ev])[-4:]
        self.rs = []


class Prog:
    ENGINES = ("sync", "vector", "scalar", "tensor", "gpsimd")

    def __init__(self):
        self.q = {e: [] for e in self.ENGINES}
        self.cnt = {}
        self.sems = {}
        self.lastw = {e: {} for e in self.ENGINES}

    def emit(self, eng, fn, deps=(), inc=None, n=1, reads=(), writes=()):
        dd = _merge(list(deps)
                    + [d for b in reads for d in b.rdeps()]
                    + [d for b in writes for d in b.wdeps()])
        for s, v in dd:
            if self.lastw[eng].get(s, 0) < v:
                self.lastw[eng][s] = v
                self.q[eng].append(
                    lambda E, s=s, v=v: E.wait_ge(self.sems[s], v))
        ev = None
        if fn is not None:
            if inc is None:
                self.q[eng].append(lambda E, fn=fn: fn(E))
            else:
                self.cnt[inc] = self.cnt.get(inc, 0) + n
                tgt = self.cnt[inc]
                self.q[eng].append(
                    lambda E, fn=fn, s=inc, n=n: fn(E).then_inc(self.sems[s], n))
                ev = (inc, tgt)
        for b in reads:
            b.note_read(ev)
        for b in writes:
            b.note_write(ev)
        return ev


def build_program(debug_taps=()):
    nc = bass.Bass("TRN2", target_bir_lowering=False)
    p = Prog()

    XOWN = nc.dram_tensor("xown", [2048, D], F32, kind="ExternalInput").ap()
    XKV = nc.dram_tensor("xkv", [S, D], F32, kind="ExternalInput").ap()
    WQ = nc.dram_tensor("wq", [D, D], F32, kind="ExternalInput").ap()
    WK = nc.dram_tensor("wk", [D, 1024], F32, kind="ExternalInput").ap()
    WV = nc.dram_tensor("wv", [D, 1024], F32, kind="ExternalInput").ap()
    WO = nc.dram_tensor("wo", [D, D], F32, kind="ExternalInput").ap()
    K0 = nc.dram_tensor("k0", [D, D], U8, kind="ExternalInput").ap()
    K1 = nc.dram_tensor("k1", [D, D], U8, kind="ExternalInput").ap()
    K2 = nc.dram_tensor("k2", [D, D], U8, kind="ExternalInput").ap()
    SVIN = nc.dram_tensor("svin", [1, 3], F32, kind="ExternalInput").ap()
    COSK = nc.dram_tensor("cosk", [128, S], BF16, kind="ExternalInput").ap()
    SINK = nc.dram_tensor("sink", [128, S], BF16, kind="ExternalInput").ap()
    COSQ = nc.dram_tensor("cosq", [128, 2048], BF16, kind="ExternalInput").ap()
    SINQ = nc.dram_tensor("sinq", [128, 2048], BF16, kind="ExternalInput").ap()
    CMV = nc.dram_tensor("cmv", [128, NT, 32], F32, kind="ExternalInput").ap()
    MSK = nc.dram_tensor("msk01", [NT, 128, 8, T], U8,
                         kind="ExternalInput").ap()
    CV1 = nc.dram_tensor("cv1", [128, 32], F32, kind="ExternalInput").ap()
    CV2 = nc.dram_tensor("cv2", [128, 32], F32, kind="ExternalInput").ap()
    IDN = nc.dram_tensor("idn", [128, 128], BF16, kind="ExternalInput").ap()
    OUT = nc.dram_tensor("out", [2048, D], F32, kind="ExternalOutput").ap()

    CW = {
        "wq": nc.dram_tensor("cwq", [D, D], BF16).ap(),
        "wk": nc.dram_tensor("cwk", [D, 1024], BF16).ap(),
        "wv": nc.dram_tensor("cwv", [D, 1024], BF16).ap(),
        "wo": nc.dram_tensor("cwo", [D, D], BF16).ap(),
        "w0": nc.dram_tensor("cw0", [D, D], BF16).ap(),
        "w1": nc.dram_tensor("cw1", [D, D], BF16).ap(),
        "w2": nc.dram_tensor("cw2", [D, D], BF16).ap(),
    }
    KS = nc.dram_tensor("ks", [G, 128, S], BF16).ap()
    VS = nc.dram_tensor("vs", [S, 1024], BF16).ap()

    taps = {}
    for name in debug_taps:
        shp = {"xnf": [128, NDC, T], "q": [128, H, T], "ctx": [128, H, T],
               "ao": [128, NDC, T], "xa": [128, NT, D], "h1": [128, NDC, T],
               "h2": [128, NDC, T], "ffo": [128, NDC, T], "kks": [128, T],
               "vvs": [128, T], "bias": [128, 32], "rr": [1, T],
               "prb": [128, T]}[name]
        taps[name] = nc.dram_tensor("tap_" + name, shp, F32,
                                    kind="ExternalOutput").ap()

    st = ExitStack()
    with st:
        def sb(name, shape, dt):
            return st.enter_context(nc.sbuf_tensor(name, shape, dt))

        def psum(name, shape, dt):
            return st.enter_context(nc.psum_tensor(name, shape, dt))

        x_t = sb("x_t", [128, 4, D], F32)
        br_t = sb("br_t", [128, 2, D], BF16)
        xnf = sb("xnf", [128, NDC, T], BF16)
        qcf = sb("qcf", [128, NDC, T], BF16)
        wpan = sb("wpan", [128, 2, NDC, 128], BF16)
        kbuf = sb("kbuf", [128, S], BF16)
        vbuf = sb("vbuf", [128, 32, 128], BF16)
        tabc = sb("tabc", [128, S], BF16)
        tabs = sb("tabs", [128, S], BF16)
        stt = sb("stt", [128, T], F32)
        msku = sb("msku", [128, 8, T], U8)
        prb = sb("prb", [128, 2, T], BF16)
        stat = sb("stat", [128, 32], F32)
        smm = sb("smm", [128, 32], F32)
        cmv_s = sb("cmv_s", [128, NT, 32], F32)
        cv1_s = sb("cv1_s", [128, 32], F32)
        cv2_s = sb("cv2_s", [128, 32], F32)
        sv = sb("sv", [128, 4], F32)
        svin_s = sb("svin_s", [1, 3], F32)
        onb = sb("onb", [128, 128], BF16)
        onf = sb("onf", [1, 128], F32)
        idn_s = sb("idn_s", [128, 128], BF16)
        kn2 = sb("kn2", [1, 16], F32)
        srow1 = sb("srow1", [1, T], F32)
        rr_b = sb("rr_b", [1, T], BF16)

        pMM = [psum("pmm0", [128, T], F32), psum("pmm1", [128, T], F32)]
        pCTX = [psum("pctx0", [128, T], F32), psum("pctx1", [128, T], F32)]
        pRS0 = psum("prs0", [128, T], F32)
        pRS = [pRS0, pRS0]
        pTR = [psum("ptr0", [128, 1024], BF16), psum("ptr1", [128, 1024], BF16)]
        pBC = psum("pbc", [128, T], F32)

        for s_name in ("dma", "gdma", "out", "pe", "dve", "act", "pool"):
            p.sems[s_name] = st.enter_context(nc.semaphore("sem_" + s_name))

        bx = [Buf() for _ in range(4)]
        bbr = [Buf() for _ in range(2)]
        bxnf = Buf()
        bqcf = Buf()
        bwpan = [Buf(), Buf()]
        bkbuf = Buf()
        bvbuf = Buf()
        btabc = Buf()
        btabs = Buf()
        bstt = Buf()
        bprb = [Buf(), Buf()]
        bmsku = Buf()
        bstat = Buf()
        bsmm = Buf()
        bsmall = Buf()
        bsrow = Buf()
        brr = Buf()
        bpMM = [Buf(), Buf()]
        bpCTX = [Buf(), Buf()]
        bpRS0 = Buf()
        bpRS = [bpRS0, bpRS0]
        bpTR = [Buf(), Buf()]
        bpBC = Buf()
        bCW = {k: Buf() for k in CW}
        bKS = [Buf() for _ in range(G)]
        bVS = Buf()

        def dma(dst, src, reads=(), writes=(), eng="sync", sem=None):
            sem = sem or ("gdma" if eng == "gpsimd" else "dma")
            return p.emit(eng, lambda E: E.dma_start(out=dst, in_=src),
                          inc=sem, n=16, reads=reads, writes=writes)

        ev = p.emit("vector", lambda E: E.memset(onb[:], 1.0), inc="dve",
                    writes=(bsmall,))
        p.emit("vector", lambda E: E.memset(onf[:], 1.0), inc="dve",
               writes=(bsmall,))
        p.emit("vector", lambda E: E.memset(stat[:, 25:26], EPS), inc="dve",
               writes=(bstat,))
        p.emit("vector", lambda E: E.memset(kn2[:], 0.0), inc="dve",
               writes=(bsmall,))
        p.emit("vector", lambda E: E.memset(kn2[0:1, 11:12], CBND), inc="dve",
               writes=(bsmall,))
        dma(tabc[:], COSK, writes=(btabc,))
        dma(tabs[:], SINK, writes=(btabs,))
        dma(cmv_s[:], CMV, writes=(bsmall,))
        dma(cv1_s[:], CV1, writes=(bsmall,))
        dma(cv2_s[:], CV2, writes=(bsmall,))
        dma(idn_s[:], IDN, writes=(bsmall,))
        dma(svin_s[:], SVIN, writes=(bsmall,))
        p.emit("tensor",
               lambda E: E.matmul(pBC[:, 0:3], onf[:], svin_s[:],
                                  start=True, stop=True),
               inc="pe", reads=(bsmall,), writes=(bpBC,))
        p.emit("vector",
               lambda E: E.tensor_scalar(out=sv[:, 0:2], in0=pBC[:, 0:2],
                                         scalar1=2.0, scalar2=None,
                                         op0=ALU.mult),
               inc="dve", reads=(bpBC,), writes=(bsmall,))
        p.emit("vector",
               lambda E: E.tensor_copy(out=sv[:, 2:3], in_=pBC[:, 2:3]),
               inc="dve", reads=(bpBC,), writes=(bsmall,))

        def xnf_slot(i):
            return xnf[:, 8 * i:8 * (i + 1), :].rearrange("p a b -> p (a b)")

        def qcf_slot(i):
            return qcf[:, 8 * i:8 * (i + 1), :].rearrange("p a b -> p (a b)")

        stage_bufs = [Buf() for _ in range(4)]
        conv_bufs = [Buf() for _ in range(4)]
        mats = [("wk", WK, 1024, None), ("wv", WV, 1024, None),
                ("wq", WQ, D, None), ("w0", K0, D, "plain"),
                ("w1", K1, D, "ei"), ("w2", K2, D, "ei"),
                ("wo", WO, D, None)]
        si = 0
        for name, src, width, conv in mats:
            for dc in range(NDC):
                sl = si % 4
                si += 1
                stg = xnf_slot(sl)[:, 0:width]
                dma(stg, src[dc * 128:(dc + 1) * 128, :],
                    eng="gpsimd", writes=(stage_bufs[sl],))
                if conv is None:
                    dma(CW[name][dc * 128:(dc + 1) * 128, :], stg,
                        reads=(stage_bufs[sl],), writes=(bCW[name],))
                else:
                    cdst = qcf_slot(sl)[:, 0:width]
                    if conv == "plain":
                        p.emit("gpsimd",
                               lambda E, o=cdst, i_=stg:
                               E.tensor_scalar(out=o, in0=i_, scalar1=2.0,
                                               scalar2=1.0, op0=ALU.mult,
                                               op1=ALU.subtract),
                               inc="pool", reads=(stage_bufs[sl],),
                               writes=(conv_bufs[sl],))
                    else:
                        p.emit("gpsimd",
                               lambda E, o=cdst, i_=stg, dc=dc:
                               E.tensor_scalar(out=o, in0=i_,
                                               scalar1=cv2_s[:, dc:dc + 1],
                                               scalar2=cv1_s[:, dc:dc + 1],
                                               op0=ALU.mult,
                                               op1=ALU.subtract),
                               inc="pool", reads=(stage_bufs[sl], bsmall),
                               writes=(conv_bufs[sl],))
                    dma(CW[name][dc * 128:(dc + 1) * 128, :], cdst,
                        reads=(conv_bufs[sl],), writes=(bCW[name],))

        for sbuf_ in stage_bufs:
            bxnf.w += sbuf_.w
            bxnf.rs += sbuf_.rs
        for cbuf_ in conv_bufs:
            bqcf.w += cbuf_.w
            bqcf.rs += cbuf_.rs
        bxnf.w = _merge(bxnf.w)
        bxnf.rs = _merge(bxnf.rs)
        bqcf.w = _merge(bqcf.w)
        bqcf.rs = _merge(bqcf.rs)

        def load_panel(cname, o0, par, owidth=128):
            src = CW[cname][:, o0:o0 + owidth].rearrange(
                "(c p) m -> p c m", p=128)
            return dma(wpan[:, par, :, 0:owidth], src,
                       reads=(bCW[cname],), writes=(bwpan[par],))

        def ln_stats(xin_ap, tc, bufs_r, dump_ap, dump_buf):
            p.emit("vector",
                   lambda E: E.reduce_sum(out=stat[:, tc:tc + 1], in_=xin_ap,
                                          axis=AX.X),
                   inc="dve", reads=bufs_r, writes=(bstat,))
            p.emit("scalar",
                   lambda E: E.activation(out=dump_ap, in_=xin_ap,
                                          func=AF.Square,
                                          accum_out=stat[:, 4 + tc:5 + tc]),
                   inc="act", reads=bufs_r, writes=(dump_buf, bstat))

        def ln_finalize(ntc):
            c = 1.0 / D
            p.emit("vector",
                   lambda E: E.tensor_scalar(out=stat[:, 8:8 + ntc],
                                             in0=stat[:, 0:ntc], scalar1=c,
                                             scalar2=None, op0=ALU.mult),
                   inc="dve", reads=(bstat,), writes=(bstat,))
            p.emit("vector",
                   lambda E: E.tensor_tensor(out=stat[:, 20:20 + ntc],
                                             in0=stat[:, 8:8 + ntc],
                                             in1=stat[:, 8:8 + ntc],
                                             op=ALU.mult),
                   inc="dve", reads=(bstat,), writes=(bstat,))
            p.emit("vector",
                   lambda E: E.scalar_tensor_tensor(
                       out=stat[:, 16:16 + ntc], in0=stat[:, 4:4 + ntc],
                       scalar=c, in1=stat[:, 20:20 + ntc], op0=ALU.mult,
                       op1=ALU.subtract),
                   inc="dve", reads=(bstat,), writes=(bstat,))
            p.emit("scalar",
                   lambda E: E.activation(out=stat[:, 20:20 + ntc],
                                          in_=stat[:, 16:16 + ntc],
                                          func=AF.Ln,
                                          bias=stat[:, 25:26]),
                   inc="act", reads=(bstat,), writes=(bstat,))
            p.emit("scalar",
                   lambda E: E.activation(out=stat[:, 12:12 + ntc],
                                          in_=stat[:, 20:20 + ntc],
                                          func=AF.Exp, scale=-0.5),
                   inc="act", reads=(bstat,), writes=(bstat,))

        def transpose_fm(src_ap_fn, dst_ap_fn, src_bufs, dst_buf):
            for g4 in range(8):
                half = g4 % 2
                htr = pTR[half][:, 0:512]
                for j in range(4):
                    dc = g4 * 4 + j
                    p.emit("tensor",
                           lambda E, dc=dc, j=j, htr=htr:
                           E.transpose(htr[:, j * 128:(j + 1) * 128],
                                       src_ap_fn(dc), idn_s[:]),
                           inc="pe" if j == 3 else None,
                           reads=src_bufs + (bsmall,), writes=(bpTR[half],))
                p.emit("vector",
                       lambda E, g4=g4, htr=htr:
                       E.tensor_copy(
                           out=dst_ap_fn(g4),
                           in_=htr.rearrange("p (a b) -> p a b", a=4)),
                       inc="dve", reads=(bpTR[half],), writes=(dst_buf,))

        def proj_fm(cname, nob, rhs_buf, rhs_ap_fn, evac_fn, pre_ev=None):
            for ob in range(nob):
                par = ob % 2
                load_panel(cname, ob * 128, par)
                deps = [] if pre_ev is None else [pre_ev]
                for dc in range(NDC):
                    p.emit("tensor",
                           lambda E, dc=dc, par=par:
                           E.matmul(pMM[par][:], wpan[:, par, dc, :],
                                    rhs_ap_fn(dc), start=(dc == 0),
                                    stop=(dc == NDC - 1)),
                           inc="pe" if dc == NDC - 1 else None,
                           deps=deps,
                           reads=(bwpan[par], rhs_buf), writes=(bpMM[par],))
                evac_fn(ob, pMM[par], par)

        def rope_evac(psum_ap, psum_buf, tab_off, dst_ap, dst_buf):
            if 'c1' in ROPS:
                p.emit("vector",
                       lambda E: E.tensor_copy(out=stt[0:64, :],
                                               in_=psum_ap[64:128, :]),
                       inc="dve", reads=(psum_buf,), writes=(bstt,))
            if 'c2' in ROPS:
                p.emit("vector",
                       lambda E: E.tensor_copy(out=stt[64:128, :],
                                               in_=psum_ap[0:64, :]),
                       inc="dve", reads=(psum_buf,), writes=(bstt,))
            ct, stn = tab_off
            if 't2' in ROPS:
                p.emit("vector",
                       lambda E: E.tensor_tensor(out=dst_ap, in0=psum_ap[:],
                                                 in1=ct, op=ALU.mult),
                       inc="dve", reads=(psum_buf, btabc), writes=(dst_buf,))
            else:
                p.emit("vector",
                       lambda E: E.tensor_copy(out=dst_ap, in_=psum_ap[:]),
                       inc="dve", reads=(psum_buf,), writes=(dst_buf,))
            if 't1m' in ROPS:
                p.emit("vector",
                       lambda E: E.tensor_tensor(out=stt[:], in0=stt[:],
                                                 in1=stn, op=ALU.mult),
                       inc="dve", reads=(bstt, btabs), writes=(bstt,))
            if 'add' in ROPS:
                p.emit("vector",
                       lambda E: E.tensor_tensor(out=dst_ap, in0=dst_ap,
                                                 in1=stt[:], op=ALU.add),
                       inc="dve", reads=(bstt, dst_buf), writes=(dst_buf,))

        for kt in range(min(NKT_RUN, NKT) if PHASES >= 1 else 0):
            t0 = kt * T
            for tc in range(4):
                xs = x_t[:, tc, :]
                dma(xs, XKV[t0 + tc * 128:t0 + (tc + 1) * 128, :],
                    writes=(bx[tc],))
                if 'ln' in P1PARTS:
                    ln_stats(xs, tc, (bx[tc],), qcf_slot(tc), bqcf)
            if 'ln' in P1PARTS:
                ln_finalize(4)
            for tc in range(4):
                sl = tc % 2
                if 'ln' in P1PARTS:
                    p.emit("vector",
                           lambda E, tc=tc, sl=sl:
                           E.tensor_scalar(out=br_t[:, sl, :],
                                           in0=x_t[:, tc, :],
                                           scalar1=stat[:, 8 + tc:9 + tc],
                                           scalar2=stat[:, 12 + tc:13 + tc],
                                           op0=ALU.subtract, op1=ALU.mult),
                           inc="dve", reads=(bx[tc], bstat), writes=(bbr[sl],))
                if 'tr' in P1PARTS:
                    transpose_fm(
                        lambda dc, sl=sl: br_t[:, sl, dc * 128:(dc + 1) * 128],
                        lambda g4, tc=tc: xnf[:, 4 * g4:4 * (g4 + 1),
                                              tc * 128:(tc + 1) * 128],
                        (bbr[sl],), bxnf)

            def k_evac(g, ps, par, t0=t0):
                kv = g % 2
                if 'rope' in KPARTS:
                    rope_evac(ps[:], bpMM[par],
                              (tabc[:, t0:t0 + T], tabs[:, t0:t0 + T]),
                              prb[:, kv, :], bprb[kv])
                else:
                    p.emit("vector",
                           lambda E, kv=kv, ps=ps:
                           E.tensor_copy(out=prb[:, kv, :], in_=ps[:]),
                           inc="dve", reads=(bpMM[par],), writes=(bprb[kv],))
                if 'norm' not in KPARTS:
                    if 'store' in KPARTS:
                        dma(KS[g, :, t0:t0 + T], prb[:, kv, :],
                            reads=(bprb[kv],), writes=(bKS[g],))
                    return
                p.emit("scalar",
                       lambda E, kv=kv: E.activation(out=qcf[:, 0, :],
                                                     in_=prb[:, kv, :],
                                                     func=AF.Square),
                       inc="act", reads=(bprb[kv],), writes=(bqcf,))
                p.emit("tensor",
                       lambda E: E.matmul(pBC[0:1, :], onb[:, 0:1],
                                          qcf[:, 0, :], start=True, stop=True),
                       inc="pe", reads=(bqcf, bsmall), writes=(bpBC,))
                p.emit("vector",
                       lambda E: E.reduce_max(out=kn2[0:1, 10:11],
                                              in_=pBC[0:1, :], axis=AX.X),
                       inc="dve", reads=(bpBC,), writes=(bsmall,))
                p.emit("vector",
                       lambda E, g=g: E.tensor_tensor(out=kn2[0:1, g:g + 1],
                                                      in0=kn2[0:1, g:g + 1],
                                                      in1=kn2[0:1, 10:11],
                                                      op=ALU.max),
                       inc="dve", reads=(bsmall,), writes=(bsmall,))
                if 'store' in KPARTS:
                    dma(KS[g, :, t0:t0 + T], prb[:, kv, :], reads=(bprb[kv],),
                        writes=(bKS[g],))

            if 'k' in P1PARTS:
                proj_fm("wk", G, bxnf, lambda dc: xnf[:, dc, :], k_evac)

            def v_evac(vb, ps, par):
                kv = vb % 2
                p.emit("vector",
                       lambda E, kv=kv, ps=ps:
                       E.tensor_copy(out=prb[:, kv, :], in_=ps[:]),
                       inc="dve", reads=(bpMM[par],), writes=(bprb[kv],))
                half = vb % 2
                htr = pTR[half][:, 0:512]
                for tc in range(4):
                    p.emit("tensor",
                           lambda E, tc=tc, kv=kv, htr=htr:
                           E.transpose(htr[:, tc * 128:(tc + 1) * 128],
                                       prb[:, kv, tc * 128:(tc + 1) * 128],
                                       idn_s[:]),
                           inc="pe" if tc == 3 else None,
                           reads=(bprb[kv], bsmall), writes=(bpTR[half],))
                p.emit("vector",
                       lambda E, vb=vb, htr=htr:
                       E.tensor_copy(
                           out=kbuf[:].rearrange("p (a b c) -> p a b c",
                                                 a=4, b=8)[:, :, vb, :],
                           in_=htr.rearrange("p (a b) -> p a b", a=4)),
                       inc="dve", reads=(bpTR[half],), writes=(bkbuf,))

            if 'v' in P1PARTS:
                proj_fm("wv", 8, bxnf, lambda dc: xnf[:, dc, :], v_evac)
            for tc in range(4 if 'v' in P1PARTS else 0):
                dma(VS[t0 + tc * 128:t0 + (tc + 1) * 128, :],
                    kbuf[:].rearrange("p (a b) -> p a b", a=4)[:, tc, :],
                    reads=(bkbuf,), writes=(bVS,))

        p.emit("vector",
               lambda E: E.reduce_max(out=kn2[0:1, 8:9], in_=kn2[0:1, 0:8],
                                      axis=AX.X),
               inc="dve", reads=(bsmall,), writes=(bsmall,))

        dma(tabc[:, 0:2048], COSQ, writes=(btabc,))
        dma(tabs[:, 0:2048], SINQ, writes=(btabs,))

        for s_ in range(NT if PHASES >= 2 else 0):
            t0 = s_ * T
            kcs = KCS[s_]
            for tc in range(4):
                xs = x_t[:, tc, :]
                dma(xs, XOWN[t0 + tc * 128:t0 + (tc + 1) * 128, :],
                    writes=(bx[tc],))
                if 'ln' in P2PARTS:
                    ln_stats(xs, tc, (bx[tc],), qcf_slot(tc), bqcf)
            if 'ln' not in P2PARTS:
                continue
            ln_finalize(4)
            for tc in range(4):
                sl = tc % 2
                p.emit("vector",
                       lambda E, tc=tc, sl=sl:
                       E.tensor_scalar(out=br_t[:, sl, :], in0=x_t[:, tc, :],
                                       scalar1=stat[:, 8 + tc:9 + tc],
                                       scalar2=stat[:, 12 + tc:13 + tc],
                                       op0=ALU.subtract, op1=ALU.mult),
                       inc="dve", reads=(bx[tc], bstat), writes=(bbr[sl],))
                transpose_fm(
                    lambda dc, sl=sl: br_t[:, sl, dc * 128:(dc + 1) * 128],
                    lambda g4, tc=tc: xnf[:, 4 * g4:4 * (g4 + 1),
                                          tc * 128:(tc + 1) * 128],
                    (bbr[sl],), bxnf)
            if "xnf" in taps and s_ == 0:
                for dc in range(NDC):
                    p.emit("vector",
                           lambda E, dc=dc: E.tensor_copy(
                               out=stt[:], in_=xnf[:, dc, :]),
                           inc="dve", reads=(bxnf,), writes=(bstt,))
                    dma(taps["xnf"][:, dc, :], stt[:],
                        reads=(bstt,), sem="out")

            dma(msku[:], MSK[s_], writes=(bmsku,))
            mskv = msku

            def q_evac(h, ps, par, s_=s_):
                rope_evac(ps[:], bpMM[par],
                          (tabc[:, s_ * T:(s_ + 1) * T],
                           tabs[:, s_ * T:(s_ + 1) * T]),
                          qcf[:, h, :], bqcf)
                p.emit("scalar",
                       lambda E, h=h: E.activation(out=prb[:, 0, :],
                                                   in_=qcf[:, h, :],
                                                   func=AF.Square),
                       inc="act", reads=(bqcf,), writes=(bprb[0],))
                p.emit("tensor",
                       lambda E, h=h: E.matmul(pBC[0:1, :], onb[:, 0:1],
                                               prb[:, 0, :], start=True,
                                               stop=True),
                       inc="pe", reads=(bprb[0], bsmall), writes=(bpBC,))
                p.emit("vector",
                       lambda E: E.reduce_max(out=kn2[0:1, 14:15],
                                              in_=pBC[0:1, :], axis=AX.X),
                       inc="dve", reads=(bpBC,), writes=(bsmall,))
                p.emit("vector",
                       lambda E: E.tensor_tensor(out=kn2[0:1, 9:10],
                                                 in0=kn2[0:1, 9:10],
                                                 in1=kn2[0:1, 14:15],
                                                 op=ALU.max),
                       inc="dve", reads=(bsmall,), writes=(bsmall,))

            p.emit("vector", lambda E: E.memset(kn2[0:1, 9:10], 0.0),
                   inc="dve", writes=(bsmall,))
            if 'q' in P2PARTS:
                proj_fm("wq", H, bxnf, lambda dc: xnf[:, dc, :], q_evac)

            if 'q' not in P2PARTS:
                p.emit("tensor",
                       lambda E: E.matmul(pBC[0:1, :], onb[:, 0:1],
                                          qcf[:, 0, :], start=True,
                                          stop=True),
                       inc="pe", reads=(bqcf, bsmall), writes=(bpBC,))
            p.emit("vector",
                   lambda E: E.tensor_tensor(out=kn2[0:1, 10:11],
                                             in0=kn2[0:1, 9:10],
                                             in1=kn2[0:1, 8:9], op=ALU.mult),
                   inc="dve", reads=(bsmall,), writes=(bsmall,))
            p.emit("scalar",
                   lambda E: E.activation(out=kn2[0:1, 12:13],
                                          in_=kn2[0:1, 10:11], func=AF.Ln),
                   inc="act", reads=(bsmall,), writes=(bsmall,))
            p.emit("scalar",
                   lambda E: E.activation(out=kn2[0:1, 13:14],
                                          in_=kn2[0:1, 12:13], func=AF.Exp,
                                          scale=0.5, bias=kn2[0:1, 11:12]),
                   inc="act", reads=(bsmall,), writes=(bsmall,))
            p.emit("tensor",
                   lambda E: E.matmul(pBC[:, 0:1], onf[:], kn2[0:1, 13:14],
                                      start=True, stop=True),
                   inc="pe", reads=(bsmall,), writes=(bpBC,))
            p.emit("vector",
                   lambda E: E.tensor_scalar(out=stat[:, 24:25],
                                             in0=pBC[:, 0:1], scalar1=-1.0,
                                             scalar2=None, op0=ALU.mult),
                   inc="dve", reads=(bpBC,), writes=(bstat,))
            p.emit("vector",
                   lambda E, s_=s_, kcs=kcs:
                   E.tensor_scalar(out=smm[:, 0:kcs],
                                   in0=cmv_s[:, s_, 0:kcs],
                                   scalar1=stat[:, 24:25], scalar2=None,
                                   op0=ALU.add),
                   inc="dve", reads=(bsmall, bstat), writes=(bsmm,))
            if "bias" in taps and s_ == 0:
                dma(taps["bias"][:, 0:32], smm[:, 0:32], reads=(bsmm,),
                    sem="out")
            if "q" in taps and s_ == 0:
                for h in range(H):
                    p.emit("vector",
                           lambda E, h=h: E.tensor_copy(
                               out=stt[:], in_=qcf[:, h, :]),
                           inc="dve", reads=(bqcf,), writes=(bstt,))
                    dma(taps["q"][:, h, :], stt[:], reads=(bstt,),
                        sem="out")

            for g in range(G if 'att' in P2PARTS else 0):
                dma(kbuf[:, 0:kcs * 128], KS[g, :, 0:kcs * 128],
                    reads=(bKS[g],), writes=(bkbuf,))
                dma(vbuf[:, 0:kcs, :],
                    VS[0:kcs * 128, g * 128:(g + 1) * 128].rearrange(
                        "(c p) m -> p c m", p=128),
                    reads=(bVS,), writes=(bvbuf,))
                for h4 in range(4):
                    h = g * 4 + h4
                    cp = h % 2
                    for kc in range(kcs):
                        sp = kc % 2
                        p.emit("tensor",
                               lambda E, kc=kc, sp=sp, h=h:
                               E.matmul(pMM[sp][:],
                                        kbuf[:, kc * 128:(kc + 1) * 128],
                                        qcf[:, h, :], start=True, stop=True),
                               inc="pe", reads=(bkbuf, bqcf),
                               writes=(bpMM[sp],))
                        p.emit("scalar",
                               lambda E, kc=kc, sp=sp:
                               E.activation(out=prb[:, sp, :],
                                            in_=pMM[sp][:], func=AF.Exp,
                                            scale=ISQ,
                                            bias=smm[:, kc:kc + 1]),
                               inc="act", reads=(bpMM[sp], bsmm),
                               writes=(bprb[sp],))
                        if kc in TRI[s_]:
                            j = TRI[s_].index(kc)
                            p.emit("gpsimd",
                                   lambda E, sp=sp, j=j:
                                   E.tensor_tensor(out=prb[:, sp, :],
                                                   in0=prb[:, sp, :],
                                                   in1=mskv[:, j, :],
                                                   op=ALU.mult),
                                   inc="pool", reads=(bprb[sp], bmsku),
                                   writes=(bprb[sp],))
                        p.emit("tensor",
                               lambda E, kc=kc, sp=sp, cp=cp:
                               E.matmul(pCTX[cp][:], vbuf[:, kc, :],
                                        prb[:, sp, :], start=(kc == 0),
                                        stop=(kc == kcs - 1)),
                               inc="pe" if kc == kcs - 1 else None,
                               reads=(bvbuf, bprb[sp]), writes=(bpCTX[cp],))
                        p.emit("tensor",
                               lambda E, kc=kc, sp=sp, cp=cp:
                               E.matmul(pRS[cp][0:1, :], onb[:, 0:1],
                                        prb[:, sp, :], start=(kc == 0),
                                        stop=(kc == kcs - 1)),
                               inc="pe" if kc == kcs - 1 else None,
                               reads=(bprb[sp], bsmall), writes=(bpRS[cp],))
                    p.emit("vector",
                           lambda E, cp=cp: E.reciprocal(out=stt[0:1, :],
                                                         in_=pRS[cp][0:1, :]),
                           inc="dve", reads=(bpRS[cp],), writes=(bstt,)),
                    p.emit("vector",
                           lambda E: E.tensor_copy(out=rr_b[:],
                                                   in_=stt[0:1, :]),
                           inc="dve", reads=(bstt,), writes=(brr,))
                    p.emit("tensor",
                           lambda E: E.matmul(pBC[:], onb[0:1, :], rr_b[:],
                                              start=True, stop=True),
                           inc="pe", reads=(brr, bsmall), writes=(bpBC,))
                    p.emit("scalar",
                           lambda E: E.activation(out=stt[:], in_=pBC[:],
                                                  func=AF.Copy),
                           inc="act", reads=(bpBC,), writes=(bstt,))
                    p.emit("vector",
                           lambda E, h=h, cp=cp:
                           E.tensor_tensor(out=qcf[:, h, :], in0=pCTX[cp][:],
                                           in1=stt[:], op=ALU.mult),
                           inc="dve", reads=(bpCTX[cp], bstt),
                           writes=(bqcf,))
            if "ctx" in taps and s_ == 0:
                for h in range(H):
                    p.emit("vector",
                           lambda E, h=h: E.tensor_copy(
                               out=stt[:], in_=qcf[:, h, :]),
                           inc="dve", reads=(bqcf,), writes=(bstt,))
                    dma(taps["ctx"][:, h, :], stt[:], reads=(bstt,),
                        sem="out")

            def branch_into_x(cname, rhs_buf, rhs_ap_fn, dst_fm_buf,
                              dst_fm_ap_fn, evac_scale, tapn=None):
                def ev(ob, ps, par):
                    if evac_scale is None:
                        p.emit("vector",
                               lambda E, ob=ob, ps=ps:
                               E.tensor_copy(out=dst_fm_ap_fn(ob), in_=ps[:]),
                               inc="dve", reads=(bpMM[par],),
                               writes=(dst_fm_buf,))
                    else:
                        p.emit("scalar",
                               lambda E, ob=ob, ps=ps:
                               E.activation(out=dst_fm_ap_fn(ob), in_=ps[:],
                                            func=AF.Copy, scale=evac_scale),
                               inc="act", reads=(bpMM[par], bsmall),
                               writes=(dst_fm_buf,))
                    p.emit("tensor",
                           lambda E, ob=ob: E.matmul(
                               pCTX[0][0:1, :], onb[:, 0:1], dst_fm_ap_fn(ob),
                               start=(ob == 0), stop=(ob == NDC - 1)),
                           inc="pe" if ob == NDC - 1 else None,
                           reads=(dst_fm_buf, bsmall), writes=(bpCTX[0],))
                    p.emit("scalar",
                           lambda E, ob=ob: E.activation(out=prb[:, 0, :],
                                                         in_=dst_fm_ap_fn(ob),
                                                         func=AF.Square),
                           inc="act", reads=(dst_fm_buf,), writes=(bprb[0],))
                    p.emit("tensor",
                           lambda E, ob=ob: E.matmul(
                               pCTX[1][0:1, :], onb[:, 0:1], prb[:, 0, :],
                               start=(ob == 0), stop=(ob == NDC - 1)),
                           inc="pe" if ob == NDC - 1 else None,
                           reads=(bprb[0], bsmall), writes=(bpCTX[1],))

                proj_fm(cname, NDC, rhs_buf, rhs_ap_fn, ev)
                if tapn and tapn in taps and s_ == 0:
                    for dc in range(NDC):
                        p.emit("vector",
                               lambda E, dc=dc: E.tensor_copy(
                                   out=stt[:], in_=dst_fm_ap_fn(dc)),
                               inc="dve", reads=(dst_fm_buf,),
                               writes=(bstt,))
                        dma(taps[tapn][:, dc, :], stt[:],
                            reads=(bstt,), sem="out")
                p.emit("vector",
                       lambda E: E.tensor_copy(out=srow1[:],
                                               in_=pCTX[0][0:1, :]),
                       inc="dve", reads=(bpCTX[0],), writes=(bsrow,))
                for tc in range(4):
                    p.emit("tensor",
                           lambda E, tc=tc: E.matmul(
                               pBC[:, tc:tc + 1],
                               srow1[0:1, tc * 128:(tc + 1) * 128],
                               onf[0:1, 0:1], start=(tc == 0),
                               stop=(tc == 3)),
                           inc="pe" if tc == 3 else None,
                           reads=(bsrow, bsmall), writes=(bpBC,))
                p.emit("vector",
                       lambda E: E.tensor_copy(out=srow1[:],
                                               in_=pCTX[1][0:1, :]),
                       inc="dve", reads=(bpCTX[1],), writes=(bsrow,))
                for tc in range(4):
                    p.emit("tensor",
                           lambda E, tc=tc: E.matmul(
                               pBC[:, 4 + tc:5 + tc],
                               srow1[0:1, tc * 128:(tc + 1) * 128],
                               onf[0:1, 0:1], start=False,
                               stop=(tc == 3)),
                           inc="pe" if tc == 3 else None,
                           reads=(bsrow, bsmall), writes=(bpBC,))
                p.emit("vector",
                       lambda E: E.tensor_copy(out=stat[:, 0:8],
                                               in_=pBC[:, 0:8]),
                       inc="dve", reads=(bpBC,), writes=(bstat,))
                ln_finalize(4)
                p.emit("vector",
                       lambda E: E.tensor_tensor(out=stat[:, 20:24],
                                                 in0=stat[:, 8:12],
                                                 in1=stat[:, 12:16],
                                                 op=ALU.mult),
                       inc="dve", reads=(bstat,), writes=(bstat,))
                for tc in range(4):
                    sl = tc % 2
                    transpose_fm(
                        lambda dc, tc=tc: dst_fm_ap_fn(dc)[
                            :, tc * 128:(tc + 1) * 128],
                        lambda g4, sl=sl: br_t[:, sl, :].rearrange(
                            "p (a b) -> p a b", a=NDC)[:, 4 * g4:4 * (g4 + 1),
                                                       :],
                        (dst_fm_buf,), bbr[sl])
                    p.emit("vector",
                           lambda E, sl=sl, tc=tc:
                           E.tensor_scalar(out=br_t[:, sl, :],
                                           in0=br_t[:, sl, :],
                                           scalar1=stat[:, 12 + tc:13 + tc],
                                           scalar2=stat[:, 20 + tc:21 + tc],
                                           op0=ALU.mult, op1=ALU.subtract),
                           inc="dve", reads=(bbr[sl], bstat),
                           writes=(bbr[sl],))
                    p.emit("vector",
                           lambda E, sl=sl, tc=tc:
                           E.tensor_tensor(out=x_t[:, tc, :],
                                           in0=x_t[:, tc, :],
                                           in1=br_t[:, sl, :], op=ALU.add),
                           inc="dve", reads=(bbr[sl], bx[tc]),
                           writes=(bx[tc],))

            if 'wo' in P2PARTS:
                branch_into_x("wo", bqcf, lambda dc: qcf[:, dc, :], bxnf,
                              lambda ob: xnf[:, ob, :], None, tapn="ao")
            if "xa" in taps and s_ == 0:
                for tc in range(4):
                    dma(taps["xa"][:, tc, :], x_t[:, tc, :], reads=(bx[tc],),
                        sem="out")

            if 'ffn' not in P2PARTS:
                for tc in range(4):
                    dma(OUT[t0 + tc * 128:t0 + (tc + 1) * 128, :],
                        x_t[:, tc, :], reads=(bx[tc],), sem="out")
                continue
            for tc in range(4):
                xs = x_t[:, tc, :]
                ln_stats(xs, tc, (bx[tc],), qcf_slot(tc), bqcf)
            ln_finalize(4)
            for tc in range(4):
                sl = tc % 2
                p.emit("vector",
                       lambda E, tc=tc, sl=sl:
                       E.tensor_scalar(out=br_t[:, sl, :], in0=x_t[:, tc, :],
                                       scalar1=stat[:, 8 + tc:9 + tc],
                                       scalar2=stat[:, 12 + tc:13 + tc],
                                       op0=ALU.subtract, op1=ALU.mult),
                       inc="dve", reads=(bx[tc], bstat), writes=(bbr[sl],))
                transpose_fm(
                    lambda dc, sl=sl: br_t[:, sl, dc * 128:(dc + 1) * 128],
                    lambda g4, tc=tc: xnf[:, 4 * g4:4 * (g4 + 1),
                                          tc * 128:(tc + 1) * 128],
                    (bbr[sl],), bxnf)

            def relu_evac(svcol, dstbuf, dst_ap_fn):
                def ev(ob, ps, par):
                    p.emit("scalar",
                           lambda E, ob=ob, ps=ps:
                           E.activation(out=dst_ap_fn(ob), in_=ps[:],
                                        func=AF.Relu,
                                        scale=sv[:, svcol:svcol + 1]),
                           inc="act", reads=(bpMM[par], bsmall),
                           writes=(dstbuf,))
                return ev

            proj_fm("w0", NDC, bxnf, lambda dc: xnf[:, dc, :],
                    relu_evac(0, bqcf, lambda ob: qcf[:, ob, :]))
            if "h1" in taps and s_ == 0:
                for dc in range(NDC):
                    p.emit("vector",
                           lambda E, dc=dc: E.tensor_copy(
                               out=stt[:], in_=qcf[:, dc, :]),
                           inc="dve", reads=(bqcf,), writes=(bstt,))
                    dma(taps["h1"][:, dc, :], stt[:], reads=(bstt,),
                        sem="out")
            proj_fm("w1", NDC, bqcf, lambda dc: qcf[:, dc, :],
                    relu_evac(1, bxnf, lambda ob: xnf[:, ob, :]))
            if 'f3' in P2PARTS:
                branch_into_x("w2", bxnf, lambda dc: xnf[:, dc, :], bqcf,
                              lambda ob: qcf[:, ob, :], sv[:, 2:3],
                              tapn="ffo")

            for tc in range(4):
                dma(OUT[t0 + tc * 128:t0 + (tc + 1) * 128, :], x_t[:, tc, :],
                    reads=(bx[tc],), sem="out")

        p.emit("sync", None, deps=[("out", p.cnt.get("out", 0))])

        with nc.Block() as block:
            @block.sync
            def _(E):
                for f in p.q["sync"]:
                    f(E)

            @block.vector
            def _(E):
                for f in p.q["vector"]:
                    f(E)

            @block.scalar
            def _(E):
                for f in p.q["scalar"]:
                    f(E)

            @block.tensor
            def _(E):
                for f in p.q["tensor"]:
                    f(E)

            @block.gpsimd
            def _(E):
                for f in p.q["gpsimd"]:
                    f(E)

    return nc


def _host_tables():
    inv_freq = (1.0 / (10000.0 ** (np.arange(0, 128, 2, dtype=np.float32)
                                   / np.float32(128)))).astype(np.float32)
    ang = np.arange(S, dtype=np.float32)[:, None] * inv_freq[None, :]
    cos_h = np.cos(ang).astype(np.float32)
    sin_h = np.sin(ang).astype(np.float32)
    cosd = np.empty((128, S), np.float32)
    sins = np.empty((128, S), np.float32)
    cosd[0:64] = cos_h.T
    cosd[64:128] = cos_h.T
    sins[0:64] = -sin_h.T
    sins[64:128] = sin_h.T
    return cosd.astype(ml_dtypes.bfloat16), sins.astype(ml_dtypes.bfloat16)


def _host_masks(half):
    q0s = Q0S[half]
    cm = np.zeros((128, NT, 32), np.float32)
    msk = np.zeros((NT, 128, 8, T), np.uint8)
    for s_ in range(NT):
        q0 = q0s[s_]
        qpos = q0 + np.arange(T)
        for kc in range(KCS[s_]):
            if kc in TRI[s_]:
                j = TRI[s_].index(kc)
                kpos = kc * 128 + np.arange(128)
                msk[s_, :, j, :] = (
                    kpos[:, None] <= qpos[None, :]).astype(np.uint8)
            elif kc > q0 // 128 + 3:
                cm[:, s_, kc] = NEG
    return cm, msk


def _prep(L0_attn_x, L0_attn_Wq, L0_attn_Wk, L0_attn_Wv, L0_attn_Wo,
          L0_ffn_k0, L0_ffn_s0, L0_ffn_k1, L0_ffn_s1, L0_ffn_k2, L0_ffn_s2):
    x = np.ascontiguousarray(np.asarray(L0_attn_x, dtype=np.float32))
    wq = np.ascontiguousarray(np.asarray(L0_attn_Wq, dtype=np.float32))
    wk = np.ascontiguousarray(np.asarray(L0_attn_Wk, dtype=np.float32))
    wv = np.ascontiguousarray(np.asarray(L0_attn_Wv, dtype=np.float32))
    wo = np.ascontiguousarray(np.asarray(L0_attn_Wo, dtype=np.float32))
    k0 = np.ascontiguousarray(np.asarray(L0_ffn_k0)).view(np.uint8)
    k1 = np.ascontiguousarray(np.asarray(L0_ffn_k1)).view(np.uint8)
    k2 = np.ascontiguousarray(np.asarray(L0_ffn_k2)).view(np.uint8)
    svin = np.stack([np.float32(L0_ffn_s0), np.float32(L0_ffn_s1),
                     np.float32(L0_ffn_s2)]).reshape(1, 3)

    cosd, sins = _host_tables()
    cv1 = np.ones((128, 32), np.float32)
    for dc in range(32):
        idx = dc * 128 + np.arange(128)
        cv1[:, dc] = np.where(idx < NE, 1.0, -4.0)
    cv2 = (2.0 * cv1).astype(np.float32)
    idn = np.eye(128, dtype=np.float32).astype(ml_dtypes.bfloat16)

    in_maps = []
    for c in range(8):
        b, half = c // 2, c % 2
        q0s = Q0S[half]
        xown = np.ascontiguousarray(
            np.concatenate([x[b, q0:q0 + T] for q0 in q0s], axis=0))
        pos = np.concatenate([q0 + np.arange(T) for q0 in q0s])
        cosq = np.ascontiguousarray(cosd[:, pos])
        sinq = np.ascontiguousarray(sins[:, pos])
        cm, msk = _host_masks(half)
        in_maps.append({
            "xown": xown, "xkv": np.ascontiguousarray(x[b]),
            "wq": wq, "wk": wk, "wv": wv, "wo": wo,
            "k0": k0, "k1": k1, "k2": k2, "svin": svin,
            "cosk": cosd, "sink": sins, "cosq": cosq, "sinq": sinq,
            "cmv": cm, "msk01": msk, "cv1": cv1, "cv2": cv2, "idn": idn,
        })

    return in_maps


def kernel(**inputs):
    in_maps = _prep(**inputs)
    nc = build_program(DEBUG_TAPS)
    res = run_bass_kernel_spmd(nc, in_maps, core_ids=list(range(8)))

    out = np.empty((B, S, D), np.float32)
    for c in range(8):
        b, half = c // 2, c % 2
        o = res.results[c]["out"]
        for s_, q0 in enumerate(Q0S[half]):
            out[b, q0:q0 + T] = o[s_ * T:(s_ + 1) * T]
    return out


if __name__ == "__main__":
    rng = np.random.default_rng(0)
    print("building program...")
    nc = build_program()
    n = nc.n_instructions() if hasattr(nc, "n_instructions") else "?"
    print("built, instructions:", n)


# version 03
# speedup vs baseline: 1.3057x; 1.3057x over previous
import math
import os
import numpy as np
import ml_dtypes
import concourse.bass as bass
import concourse.mybir as mybir
from concourse.bass_utils import run_bass_kernel_spmd
from contextlib import ExitStack

F32 = mybir.dt.float32
BF16 = mybir.dt.bfloat16
U8 = mybir.dt.uint8
AF = mybir.ActivationFunctionType
ALU = mybir.AluOpType
AX = mybir.AxisListType

D = 4096
S = 4096
B = 4
H = 32
G = 8
HD = 128
T = 512
NT = 4
NKT = 8
NDC = 32
NE = 3277
EPS = 1e-5
ISQ = 1.0 / math.sqrt(128.0)
CBND = math.log(1.05) - 0.5 * math.log(128.0)
NEG = -1.0e9

KCS = [12, 16, 28, 32]
TRI = [list(range(0, 4)) + list(range(8, 12)),
       list(range(4, 8)) + list(range(12, 16)),
       list(range(16, 20)) + list(range(24, 28)),
       list(range(20, 24)) + list(range(28, 32))]
Q0S = {0: [0, 512, 3072, 3584], 1: [1024, 1536, 2048, 2560]}

PHASES = int(os.environ.get('ATH_PHASES', '2'))
NKT_RUN = int(os.environ.get('ATH_KT', '8'))
P1PARTS = set(os.environ.get('ATH_P1', 'ln,tr,k,v').split(','))
P2PARTS = set(os.environ.get('ATH_P2', 'ln,q,att,wo,ffn,f3').split(','))
KPARTS = set(os.environ.get('ATH_KP', 'rope,norm,store').split(','))
ROPS = set(os.environ.get('ATH_ROPE', 'c1,c2,t2,t1m,add').split(','))
DEBUG_TAPS = ()


def _merge(deps):
    m = {}
    for d in deps:
        if d is None:
            continue
        s, v = d
        if m.get(s, -1) < v:
            m[s] = v
    return list(m.items())


class Buf:
    __slots__ = ("w", "rs")

    def __init__(self):
        self.w = []
        self.rs = []

    def rdeps(self):
        return list(self.w)

    def wdeps(self):
        return list(self.w) + list(self.rs)

    def note_read(self, ev):
        if ev is not None:
            self.rs.append(ev)
            if len(self.rs) > 8:
                self.rs = _merge(self.rs)

    def note_write(self, ev):
        self.w = [] if ev is None else _merge(self.w + [ev])
        self.rs = []


class RegionBuf:

    def __init__(self, n):
        self.bufs = [Buf() for _ in range(n)]

    def r(self, i, k=1):
        return self.bufs[i:i + k]

    def all(self):
        return self.bufs


class Prog:
    ENGINES = ("sync", "vector", "scalar", "tensor", "gpsimd")

    def __init__(self):
        self.q = {e: [] for e in self.ENGINES}
        self.cnt = {}
        self.sems = {}
        self.lastw = {e: {} for e in self.ENGINES}

    def emit(self, eng, fn, deps=(), inc=None, n=1, reads=(), writes=()):
        dd = _merge(list(deps)
                    + [d for b in reads for d in b.rdeps()]
                    + [d for b in writes for d in b.wdeps()])
        for s, v in dd:
            if self.lastw[eng].get(s, 0) < v:
                self.lastw[eng][s] = v
                self.q[eng].append(
                    lambda E, s=s, v=v: E.wait_ge(self.sems[s], v))
        ev = None
        if fn is not None:
            if inc is None:
                self.q[eng].append(lambda E, fn=fn: fn(E))
            else:
                self.cnt[inc] = self.cnt.get(inc, 0) + n
                tgt = self.cnt[inc]
                self.q[eng].append(
                    lambda E, fn=fn, s=inc, n=n: fn(E).then_inc(self.sems[s], n))
                ev = (inc, tgt)
        for b in reads:
            b.note_read(ev)
        for b in writes:
            b.note_write(ev)
        return ev


def build_program(debug_taps=()):
    nc = bass.Bass("TRN2", target_bir_lowering=False)
    p = Prog()

    XOWN = nc.dram_tensor("xown", [2048, D], F32, kind="ExternalInput").ap()
    XKV = nc.dram_tensor("xkv", [S, D], F32, kind="ExternalInput").ap()
    WQ = nc.dram_tensor("wq", [D, D], F32, kind="ExternalInput").ap()
    WK = nc.dram_tensor("wk", [D, 1024], F32, kind="ExternalInput").ap()
    WV = nc.dram_tensor("wv", [D, 1024], F32, kind="ExternalInput").ap()
    WO = nc.dram_tensor("wo", [D, D], F32, kind="ExternalInput").ap()
    K0 = nc.dram_tensor("k0", [D, D], U8, kind="ExternalInput").ap()
    K1 = nc.dram_tensor("k1", [D, D], U8, kind="ExternalInput").ap()
    K2 = nc.dram_tensor("k2", [D, D], U8, kind="ExternalInput").ap()
    SVIN = nc.dram_tensor("svin", [1, 3], F32, kind="ExternalInput").ap()
    COSK = nc.dram_tensor("cosk", [128, S], BF16, kind="ExternalInput").ap()
    SINK = nc.dram_tensor("sink", [128, S], BF16, kind="ExternalInput").ap()
    COSQ = nc.dram_tensor("cosq", [128, 2048], BF16, kind="ExternalInput").ap()
    SINQ = nc.dram_tensor("sinq", [128, 2048], BF16, kind="ExternalInput").ap()
    CMV = nc.dram_tensor("cmv", [128, NT, 32], F32, kind="ExternalInput").ap()
    MSK = nc.dram_tensor("msk01", [NT, 128, 8, T], U8,
                         kind="ExternalInput").ap()
    CV1 = nc.dram_tensor("cv1", [128, 32], F32, kind="ExternalInput").ap()
    CV2 = nc.dram_tensor("cv2", [128, 32], F32, kind="ExternalInput").ap()
    IDN = nc.dram_tensor("idn", [128, 128], BF16, kind="ExternalInput").ap()
    ROT = nc.dram_tensor("rot", [128, 128], F32, kind="ExternalInput").ap()
    OUT = nc.dram_tensor("out", [2048, D], F32, kind="ExternalOutput").ap()

    CW = {
        "wq": nc.dram_tensor("cwq", [D, D], BF16).ap(),
        "wk": nc.dram_tensor("cwk", [D, 1024], BF16).ap(),
        "wv": nc.dram_tensor("cwv", [D, 1024], BF16).ap(),
        "wo": nc.dram_tensor("cwo", [D, D], BF16).ap(),
        "w0": nc.dram_tensor("cw0", [D, D], BF16).ap(),
        "w1": nc.dram_tensor("cw1", [D, D], BF16).ap(),
        "w2": nc.dram_tensor("cw2", [D, D], BF16).ap(),
    }
    KS = nc.dram_tensor("ks", [G, 128, S], BF16).ap()
    VS = nc.dram_tensor("vs", [S, 1024], BF16).ap()

    taps = {}
    for name in debug_taps:
        shp = {"xnf": [128, NDC, T], "q": [128, H, T], "ctx": [128, H, T],
               "ao": [128, NDC, T], "xa": [128, NT, D], "h1": [128, NDC, T],
               "h2": [128, NDC, T], "ffo": [128, NDC, T], "kks": [128, T],
               "vvs": [128, T], "bias": [128, 32], "rr": [1, T],
               "prb": [128, T]}[name]
        taps[name] = nc.dram_tensor("tap_" + name, shp, F32,
                                    kind="ExternalOutput").ap()

    st = ExitStack()
    with st:
        def sb(name, shape, dt):
            return st.enter_context(nc.sbuf_tensor(name, shape, dt))

        def psum(name, shape, dt):
            return st.enter_context(nc.psum_tensor(name, shape, dt))

        x_t = sb("x_t", [128, 4, D], F32)
        br_t = sb("br_t", [128, 2, D], BF16)
        xnf = sb("xnf", [128, NDC, T], BF16)
        qcf = sb("qcf", [128, NDC, T], BF16)
        wpan = sb("wpan", [128, 2, NDC, 128], BF16)
        kbuf = sb("kbuf", [128, S], BF16)
        vbuf = sb("vbuf", [128, 32, 128], BF16)
        tabc = sb("tabc", [128, S], BF16)
        tabs = sb("tabs", [128, S], BF16)
        stt = sb("stt", [128, T], F32)
        msku = sb("msku", [128, 8, T], U8)
        prb = sb("prb", [128, 2, T], BF16)
        stat = sb("stat", [128, 32], F32)
        smm = sb("smm", [128, 32], F32)
        cmv_s = sb("cmv_s", [128, NT, 32], F32)
        cv1_s = sb("cv1_s", [128, 32], F32)
        cv2_s = sb("cv2_s", [128, 32], F32)
        sv = sb("sv", [128, 4], F32)
        svin_s = sb("svin_s", [1, 3], F32)
        onb = sb("onb", [128, 128], BF16)
        onf = sb("onf", [1, 128], F32)
        idn_s = sb("idn_s", [128, 128], BF16)
        rot_s = sb("rot_s", [128, 128], F32)
        kn2 = sb("kn2", [1, 16], F32)
        srow1 = sb("srow1", [1, T], F32)
        rr_b = sb("rr_b", [1, T], BF16)

        pMM = [psum("pmm0", [128, T], F32), psum("pmm1", [128, T], F32)]
        pCTX = [psum("pctx0", [128, T], F32), psum("pctx1", [128, T], F32)]
        pRS0 = psum("prs0", [128, T], F32)
        pRS = [pRS0, pRS0]
        pTR = [psum("ptr0", [128, 1024], BF16), psum("ptr1", [128, 1024], BF16)]
        pBC = psum("pbc", [128, T], F32)

        for s_name in ("dma", "gdma", "out", "pe", "dve", "act", "pool"):
            p.sems[s_name] = st.enter_context(nc.semaphore("sem_" + s_name))

        bx = [Buf() for _ in range(4)]
        bbr = [Buf() for _ in range(2)]
        bxnf = Buf()
        bqcfR = RegionBuf(NDC)
        bwpan = [Buf(), Buf()]
        bkbuf = Buf()
        bvbuf = Buf()
        btabc = Buf()
        btabs = Buf()
        bstt = Buf()
        bprb = [Buf(), Buf()]
        bmsku = Buf()
        bstat = Buf()
        bsmm = Buf()
        bsmall = Buf()
        bsrow = Buf()
        brr = Buf()
        bpMM = [Buf(), Buf()]
        bpCTX = [Buf(), Buf()]
        bpRS0 = Buf()
        bpRS = [bpRS0, bpRS0]
        bpTR = [Buf(), Buf()]
        bpBC = Buf()
        bCW = {k: Buf() for k in CW}
        bKS = [Buf() for _ in range(G)]
        bVS = Buf()

        def dma(dst, src, reads=(), writes=(), eng="sync", sem=None):
            sem = sem or ("gdma" if eng == "gpsimd" else "dma")
            return p.emit(eng, lambda E: E.dma_start(out=dst, in_=src),
                          inc=sem, n=16, reads=reads, writes=writes)

        ev = p.emit("vector", lambda E: E.memset(onb[:], 1.0), inc="dve",
                    writes=(bsmall,))
        p.emit("vector", lambda E: E.memset(onf[:], 1.0), inc="dve",
               writes=(bsmall,))
        p.emit("vector", lambda E: E.memset(stat[:, 25:26], EPS), inc="dve",
               writes=(bstat,))
        p.emit("vector", lambda E: E.memset(kn2[:], 0.0), inc="dve",
               writes=(bsmall,))
        p.emit("vector", lambda E: E.memset(kn2[0:1, 11:12], CBND), inc="dve",
               writes=(bsmall,))
        dma(tabc[:], COSK, writes=(btabc,))
        dma(tabs[:], SINK, writes=(btabs,))
        dma(cmv_s[:], CMV, writes=(bsmall,))
        dma(cv1_s[:], CV1, writes=(bsmall,))
        dma(cv2_s[:], CV2, writes=(bsmall,))
        dma(idn_s[:], IDN, writes=(bsmall,))
        dma(rot_s[:], ROT, writes=(bsmall,))
        dma(svin_s[:], SVIN, writes=(bsmall,))
        p.emit("tensor",
               lambda E: E.matmul(pBC[:, 0:3], onf[:], svin_s[:],
                                  start=True, stop=True),
               inc="pe", reads=(bsmall,), writes=(bpBC,))
        p.emit("vector",
               lambda E: E.tensor_scalar(out=sv[:, 0:2], in0=pBC[:, 0:2],
                                         scalar1=2.0, scalar2=None,
                                         op0=ALU.mult),
               inc="dve", reads=(bpBC,), writes=(bsmall,))
        p.emit("vector",
               lambda E: E.tensor_copy(out=sv[:, 2:3], in_=pBC[:, 2:3]),
               inc="dve", reads=(bpBC,), writes=(bsmall,))

        def xnf_slot(i):
            return xnf[:, 8 * i:8 * (i + 1), :].rearrange("p a b -> p (a b)")

        def qcf_slot(i):
            return qcf[:, 8 * i:8 * (i + 1), :].rearrange("p a b -> p (a b)")

        stage_bufs = [Buf() for _ in range(4)]
        conv_bufs = [Buf() for _ in range(4)]
        mats = [("wk", WK, 1024, None), ("wv", WV, 1024, None),
                ("wq", WQ, D, None), ("w0", K0, D, "plain"),
                ("w1", K1, D, "ei"), ("w2", K2, D, "ei"),
                ("wo", WO, D, None)]
        si = 0
        for name, src, width, conv in mats:
            for dc in range(NDC):
                sl = si % 4
                si += 1
                stg = xnf_slot(sl)[:, 0:width]
                dma(stg, src[dc * 128:(dc + 1) * 128, :],
                    eng="gpsimd", writes=(stage_bufs[sl],))
                if conv is None:
                    dma(CW[name][dc * 128:(dc + 1) * 128, :], stg,
                        reads=(stage_bufs[sl],), writes=(bCW[name],))
                else:
                    cdst = qcf_slot(sl)[:, 0:width]
                    if conv == "plain":
                        p.emit("vector",
                               lambda E, o=cdst, i_=stg:
                               E.tensor_scalar(out=o, in0=i_, scalar1=2.0,
                                               scalar2=1.0, op0=ALU.mult,
                                               op1=ALU.subtract),
                               inc="dve", reads=(stage_bufs[sl],),
                               writes=(conv_bufs[sl],))
                    else:
                        p.emit("vector",
                               lambda E, o=cdst, i_=stg, dc=dc:
                               E.tensor_scalar(out=o, in0=i_,
                                               scalar1=cv2_s[:, dc:dc + 1],
                                               scalar2=cv1_s[:, dc:dc + 1],
                                               op0=ALU.mult,
                                               op1=ALU.subtract),
                               inc="dve", reads=(stage_bufs[sl], bsmall),
                               writes=(conv_bufs[sl],))
                    dma(CW[name][dc * 128:(dc + 1) * 128, :], cdst,
                        reads=(conv_bufs[sl],), writes=(bCW[name],))

        for sbuf_ in stage_bufs:
            bxnf.w += sbuf_.w
            bxnf.rs += sbuf_.rs
        for qb in bqcfR.all():
            for cbuf_ in conv_bufs:
                qb.w += cbuf_.w
                qb.rs += cbuf_.rs
            qb.w = _merge(qb.w)
            qb.rs = _merge(qb.rs)
        bxnf.w = _merge(bxnf.w)
        bxnf.rs = _merge(bxnf.rs)

        def _rb(b, i):
            return bqcfR.bufs[i] if b is bqcfR else b

        def _rball(b):
            return tuple(b.bufs) if b is bqcfR else (b,)

        def load_panel(cname, o0, par, owidth=128):
            src = CW[cname][:, o0:o0 + owidth].rearrange(
                "(c p) m -> p c m", p=128)
            return dma(wpan[:, par, :, 0:owidth], src,
                       reads=(bCW[cname],), writes=(bwpan[par],))

        def ln_stats(xin_ap, tc, bufs_r, dump_ap, dump_bufs):
            p.emit("vector",
                   lambda E: E.reduce_sum(out=stat[:, tc:tc + 1], in_=xin_ap,
                                          axis=AX.X),
                   inc="dve", reads=bufs_r, writes=(bstat,))
            p.emit("scalar",
                   lambda E: E.activation(out=dump_ap, in_=xin_ap,
                                          func=AF.Square,
                                          accum_out=stat[:, 4 + tc:5 + tc]),
                   inc="act", reads=bufs_r, writes=tuple(dump_bufs) + (bstat,))

        def ln_finalize(ntc):
            c = 1.0 / D
            p.emit("vector",
                   lambda E: E.tensor_scalar(out=stat[:, 8:8 + ntc],
                                             in0=stat[:, 0:ntc], scalar1=c,
                                             scalar2=None, op0=ALU.mult),
                   inc="dve", reads=(bstat,), writes=(bstat,))
            p.emit("vector",
                   lambda E: E.tensor_tensor(out=stat[:, 20:20 + ntc],
                                             in0=stat[:, 8:8 + ntc],
                                             in1=stat[:, 8:8 + ntc],
                                             op=ALU.mult),
                   inc="dve", reads=(bstat,), writes=(bstat,))
            p.emit("vector",
                   lambda E: E.scalar_tensor_tensor(
                       out=stat[:, 16:16 + ntc], in0=stat[:, 4:4 + ntc],
                       scalar=c, in1=stat[:, 20:20 + ntc], op0=ALU.mult,
                       op1=ALU.subtract),
                   inc="dve", reads=(bstat,), writes=(bstat,))
            p.emit("scalar",
                   lambda E: E.activation(out=stat[:, 20:20 + ntc],
                                          in_=stat[:, 16:16 + ntc],
                                          func=AF.Ln,
                                          bias=stat[:, 25:26]),
                   inc="act", reads=(bstat,), writes=(bstat,))
            p.emit("scalar",
                   lambda E: E.activation(out=stat[:, 12:12 + ntc],
                                          in_=stat[:, 20:20 + ntc],
                                          func=AF.Exp, scale=-0.5),
                   inc="act", reads=(bstat,), writes=(bstat,))

        def transpose_fm(src_ap_fn, dst_ap_fn, src_bufs, dst_buf):
            for g4 in range(8):
                half = g4 % 2
                htr = pTR[half][:, 0:512]
                for j in range(4):
                    dc = g4 * 4 + j
                    p.emit("tensor",
                           lambda E, dc=dc, j=j, htr=htr:
                           E.transpose(htr[:, j * 128:(j + 1) * 128],
                                       src_ap_fn(dc), idn_s[:]),
                           inc="pe" if j == 3 else None,
                           reads=src_bufs + (bsmall,), writes=(bpTR[half],))
                p.emit("vector",
                       lambda E, g4=g4, htr=htr:
                       E.tensor_copy(
                           out=dst_ap_fn(g4),
                           in_=htr.rearrange("p (a b) -> p a b", a=4)),
                       inc="dve", reads=(bpTR[half],), writes=(dst_buf,))

        def proj_fm(cname, nob, rhs_buf, rhs_ap_fn, evac_fn, pre_ev=None):
            for ob in range(nob):
                par = ob % 2
                load_panel(cname, ob * 128, par)
                deps = [] if pre_ev is None else [pre_ev]
                for dc in range(NDC):
                    p.emit("tensor",
                           lambda E, dc=dc, par=par:
                           E.matmul(pMM[par][:], wpan[:, par, dc, :],
                                    rhs_ap_fn(dc), start=(dc == 0),
                                    stop=(dc == NDC - 1)),
                           inc="pe" if dc == NDC - 1 else None,
                           deps=deps,
                           reads=(bwpan[par], _rb(rhs_buf, dc)),
                           writes=(bpMM[par],))
                evac_fn(ob, pMM[par], par)

        def rope_evac(psum_ap, psum_buf, tab_off, dst_ap, dst_buf, rpar=0):
            ct, stn = tab_off
            p.emit("vector",
                   lambda E: E.tensor_copy(out=stt[:], in_=psum_ap[:]),
                   inc="dve", reads=(psum_buf,), writes=(bstt,))
            p.emit("tensor",
                   lambda E: E.matmul(pCTX[rpar][:], rot_s[:], stt[:],
                                      start=True, stop=True),
                   inc="pe", reads=(bstt, bsmall), writes=(bpCTX[rpar],))
            p.emit("vector",
                   lambda E: E.tensor_tensor(out=dst_ap, in0=psum_ap[:],
                                             in1=ct, op=ALU.mult),
                   inc="dve", reads=(psum_buf, btabc), writes=(dst_buf,))
            p.emit("vector",
                   lambda E: E.tensor_tensor(out=stt[:], in0=pCTX[rpar][:],
                                             in1=stn, op=ALU.mult),
                   inc="dve", reads=(bpCTX[rpar], btabs), writes=(bstt,))
            p.emit("vector",
                   lambda E: E.tensor_tensor(out=dst_ap, in0=dst_ap,
                                             in1=stt[:], op=ALU.add),
                   inc="dve", reads=(bstt, dst_buf), writes=(dst_buf,))

        for kt in range(min(NKT_RUN, NKT) if PHASES >= 1 else 0):
            t0 = kt * T
            for tc in range(4):
                xs = x_t[:, tc, :]
                dma(xs, XKV[t0 + tc * 128:t0 + (tc + 1) * 128, :],
                    writes=(bx[tc],))
                if 'ln' in P1PARTS:
                    ln_stats(xs, tc, (bx[tc],), qcf_slot(tc), bqcfR.r(8 * tc, 8))
            if 'ln' in P1PARTS:
                ln_finalize(4)
            for tc in range(4):
                sl = tc % 2
                if 'ln' in P1PARTS:
                    p.emit("vector",
                           lambda E, tc=tc, sl=sl:
                           E.tensor_scalar(out=br_t[:, sl, :],
                                           in0=x_t[:, tc, :],
                                           scalar1=stat[:, 8 + tc:9 + tc],
                                           scalar2=stat[:, 12 + tc:13 + tc],
                                           op0=ALU.subtract, op1=ALU.mult),
                           inc="dve", reads=(bx[tc], bstat), writes=(bbr[sl],))
                if 'tr' in P1PARTS:
                    transpose_fm(
                        lambda dc, sl=sl: br_t[:, sl, dc * 128:(dc + 1) * 128],
                        lambda g4, tc=tc: xnf[:, 4 * g4:4 * (g4 + 1),
                                              tc * 128:(tc + 1) * 128],
                        (bbr[sl],), bxnf)

            def k_evac(g, ps, par, t0=t0):
                kv = g % 2
                if 'rope' in KPARTS:
                    rope_evac(ps[:], bpMM[par],
                              (tabc[:, t0:t0 + T], tabs[:, t0:t0 + T]),
                              prb[:, kv, :], bprb[kv], rpar=par)
                else:
                    p.emit("vector",
                           lambda E, kv=kv, ps=ps:
                           E.tensor_copy(out=prb[:, kv, :], in_=ps[:]),
                           inc="dve", reads=(bpMM[par],), writes=(bprb[kv],))
                if 'norm' not in KPARTS:
                    if 'store' in KPARTS:
                        dma(KS[g, :, t0:t0 + T], prb[:, kv, :],
                            reads=(bprb[kv],), writes=(bKS[g],))
                    return
                p.emit("scalar",
                       lambda E, kv=kv: E.activation(out=qcf[:, 0, :],
                                                     in_=prb[:, kv, :],
                                                     func=AF.Square),
                       inc="act", reads=(bprb[kv],),
                       writes=(bqcfR.bufs[0],))
                p.emit("tensor",
                       lambda E: E.matmul(pBC[0:1, :], onb[:, 0:1],
                                          qcf[:, 0, :], start=True, stop=True),
                       inc="pe", reads=(bqcfR.bufs[0], bsmall),
                       writes=(bpBC,))
                p.emit("vector",
                       lambda E: E.reduce_max(out=kn2[0:1, 10:11],
                                              in_=pBC[0:1, :], axis=AX.X),
                       inc="dve", reads=(bpBC,), writes=(bsmall,))
                p.emit("vector",
                       lambda E, g=g: E.tensor_tensor(out=kn2[0:1, g:g + 1],
                                                      in0=kn2[0:1, g:g + 1],
                                                      in1=kn2[0:1, 10:11],
                                                      op=ALU.max),
                       inc="dve", reads=(bsmall,), writes=(bsmall,))
                if 'store' in KPARTS:
                    dma(KS[g, :, t0:t0 + T], prb[:, kv, :], reads=(bprb[kv],),
                        writes=(bKS[g],))

            if 'k' in P1PARTS:
                proj_fm("wk", G, bxnf, lambda dc: xnf[:, dc, :], k_evac)

            def v_evac(vb, ps, par):
                kv = vb % 2
                p.emit("vector",
                       lambda E, kv=kv, ps=ps:
                       E.tensor_copy(out=prb[:, kv, :], in_=ps[:]),
                       inc="dve", reads=(bpMM[par],), writes=(bprb[kv],))
                half = vb % 2
                htr = pTR[half][:, 0:512]
                for tc in range(4):
                    p.emit("tensor",
                           lambda E, tc=tc, kv=kv, htr=htr:
                           E.transpose(htr[:, tc * 128:(tc + 1) * 128],
                                       prb[:, kv, tc * 128:(tc + 1) * 128],
                                       idn_s[:]),
                           inc="pe" if tc == 3 else None,
                           reads=(bprb[kv], bsmall), writes=(bpTR[half],))
                p.emit("vector",
                       lambda E, vb=vb, htr=htr:
                       E.tensor_copy(
                           out=kbuf[:].rearrange("p (a b c) -> p a b c",
                                                 a=4, b=8)[:, :, vb, :],
                           in_=htr.rearrange("p (a b) -> p a b", a=4)),
                       inc="dve", reads=(bpTR[half],), writes=(bkbuf,))

            if 'v' in P1PARTS:
                proj_fm("wv", 8, bxnf, lambda dc: xnf[:, dc, :], v_evac)
            for tc in range(4 if 'v' in P1PARTS else 0):
                dma(VS[t0 + tc * 128:t0 + (tc + 1) * 128, :],
                    kbuf[:].rearrange("p (a b) -> p a b", a=4)[:, tc, :],
                    reads=(bkbuf,), writes=(bVS,))

        p.emit("vector",
               lambda E: E.reduce_max(out=kn2[0:1, 8:9], in_=kn2[0:1, 0:8],
                                      axis=AX.X),
               inc="dve", reads=(bsmall,), writes=(bsmall,))

        dma(tabc[:, 0:2048], COSQ, writes=(btabc,))
        dma(tabs[:, 0:2048], SINQ, writes=(btabs,))

        for s_ in range(NT if PHASES >= 2 else 0):
            t0 = s_ * T
            kcs = KCS[s_]
            for tc in range(4):
                xs = x_t[:, tc, :]
                dma(xs, XOWN[t0 + tc * 128:t0 + (tc + 1) * 128, :],
                    writes=(bx[tc],))
                if 'ln' in P2PARTS:
                    ln_stats(xs, tc, (bx[tc],), qcf_slot(tc), bqcfR.r(8 * tc, 8))
            if 'ln' not in P2PARTS:
                continue
            ln_finalize(4)
            for tc in range(4):
                sl = tc % 2
                p.emit("vector",
                       lambda E, tc=tc, sl=sl:
                       E.tensor_scalar(out=br_t[:, sl, :], in0=x_t[:, tc, :],
                                       scalar1=stat[:, 8 + tc:9 + tc],
                                       scalar2=stat[:, 12 + tc:13 + tc],
                                       op0=ALU.subtract, op1=ALU.mult),
                       inc="dve", reads=(bx[tc], bstat), writes=(bbr[sl],))
                transpose_fm(
                    lambda dc, sl=sl: br_t[:, sl, dc * 128:(dc + 1) * 128],
                    lambda g4, tc=tc: xnf[:, 4 * g4:4 * (g4 + 1),
                                          tc * 128:(tc + 1) * 128],
                    (bbr[sl],), bxnf)
            if "xnf" in taps and s_ == 0:
                for dc in range(NDC):
                    p.emit("vector",
                           lambda E, dc=dc: E.tensor_copy(
                               out=stt[:], in_=xnf[:, dc, :]),
                           inc="dve", reads=(bxnf,), writes=(bstt,))
                    dma(taps["xnf"][:, dc, :], stt[:],
                        reads=(bstt,), sem="out")

            dma(msku[:], MSK[s_], writes=(bmsku,))
            mskv = msku

            def q_evac(h, ps, par, s_=s_):
                rope_evac(ps[:], bpMM[par],
                          (tabc[:, s_ * T:(s_ + 1) * T],
                           tabs[:, s_ * T:(s_ + 1) * T]),
                          qcf[:, h, :], bqcfR.bufs[h], rpar=par)
                p.emit("scalar",
                       lambda E, h=h: E.activation(out=prb[:, 0, :],
                                                   in_=qcf[:, h, :],
                                                   func=AF.Square),
                       inc="act", reads=(bqcfR.bufs[h],), writes=(bprb[0],))
                p.emit("tensor",
                       lambda E, h=h: E.matmul(pBC[0:1, :], onb[:, 0:1],
                                               prb[:, 0, :], start=True,
                                               stop=True),
                       inc="pe", reads=(bprb[0], bsmall), writes=(bpBC,))
                p.emit("vector",
                       lambda E: E.reduce_max(out=kn2[0:1, 14:15],
                                              in_=pBC[0:1, :], axis=AX.X),
                       inc="dve", reads=(bpBC,), writes=(bsmall,))
                p.emit("vector",
                       lambda E: E.tensor_tensor(out=kn2[0:1, 9:10],
                                                 in0=kn2[0:1, 9:10],
                                                 in1=kn2[0:1, 14:15],
                                                 op=ALU.max),
                       inc="dve", reads=(bsmall,), writes=(bsmall,))

            p.emit("vector", lambda E: E.memset(kn2[0:1, 9:10], 0.0),
                   inc="dve", writes=(bsmall,))
            if 'q' in P2PARTS:
                proj_fm("wq", H, bxnf, lambda dc: xnf[:, dc, :], q_evac)

            if 'q' not in P2PARTS:
                p.emit("tensor",
                       lambda E: E.matmul(pBC[0:1, :], onb[:, 0:1],
                                          qcf[:, 0, :], start=True,
                                          stop=True),
                       inc="pe", reads=(bqcfR.bufs[0], bsmall),
                       writes=(bpBC,))
            p.emit("vector",
                   lambda E: E.tensor_tensor(out=kn2[0:1, 10:11],
                                             in0=kn2[0:1, 9:10],
                                             in1=kn2[0:1, 8:9], op=ALU.mult),
                   inc="dve", reads=(bsmall,), writes=(bsmall,))
            p.emit("scalar",
                   lambda E: E.activation(out=kn2[0:1, 12:13],
                                          in_=kn2[0:1, 10:11], func=AF.Ln),
                   inc="act", reads=(bsmall,), writes=(bsmall,))
            p.emit("scalar",
                   lambda E: E.activation(out=kn2[0:1, 13:14],
                                          in_=kn2[0:1, 12:13], func=AF.Exp,
                                          scale=0.5, bias=kn2[0:1, 11:12]),
                   inc="act", reads=(bsmall,), writes=(bsmall,))
            p.emit("tensor",
                   lambda E: E.matmul(pBC[:, 0:1], onf[:], kn2[0:1, 13:14],
                                      start=True, stop=True),
                   inc="pe", reads=(bsmall,), writes=(bpBC,))
            p.emit("vector",
                   lambda E: E.tensor_scalar(out=stat[:, 24:25],
                                             in0=pBC[:, 0:1], scalar1=-1.0,
                                             scalar2=None, op0=ALU.mult),
                   inc="dve", reads=(bpBC,), writes=(bstat,))
            p.emit("vector",
                   lambda E, s_=s_, kcs=kcs:
                   E.tensor_scalar(out=smm[:, 0:kcs],
                                   in0=cmv_s[:, s_, 0:kcs],
                                   scalar1=stat[:, 24:25], scalar2=None,
                                   op0=ALU.add),
                   inc="dve", reads=(bsmall, bstat), writes=(bsmm,))
            if "bias" in taps and s_ == 0:
                dma(taps["bias"][:, 0:32], smm[:, 0:32], reads=(bsmm,),
                    sem="out")
            if "q" in taps and s_ == 0:
                for h in range(H):
                    p.emit("vector",
                           lambda E, h=h: E.tensor_copy(
                               out=stt[:], in_=qcf[:, h, :]),
                           inc="dve", reads=(bqcfR.bufs[h],), writes=(bstt,))
                    dma(taps["q"][:, h, :], stt[:], reads=(bstt,),
                        sem="out")

            for g in range(G if 'att' in P2PARTS else 0):
                dma(kbuf[:, 0:kcs * 128], KS[g, :, 0:kcs * 128],
                    reads=(bKS[g],), writes=(bkbuf,))
                dma(vbuf[:, 0:kcs, :],
                    VS[0:kcs * 128, g * 128:(g + 1) * 128].rearrange(
                        "(c p) m -> p c m", p=128),
                    reads=(bVS,), writes=(bvbuf,))
                for h4 in range(4):
                    h = g * 4 + h4
                    cp = h % 2
                    for kc in range(kcs):
                        sp = kc % 2
                        p.emit("tensor",
                               lambda E, kc=kc, sp=sp, h=h:
                               E.matmul(pMM[sp][:],
                                        kbuf[:, kc * 128:(kc + 1) * 128],
                                        qcf[:, h, :], start=True, stop=True),
                               inc="pe", reads=(bkbuf, bqcfR.bufs[h]),
                               writes=(bpMM[sp],))
                        p.emit("scalar",
                               lambda E, kc=kc, sp=sp:
                               E.activation(out=prb[:, sp, :],
                                            in_=pMM[sp][:], func=AF.Exp,
                                            scale=ISQ,
                                            bias=smm[:, kc:kc + 1]),
                               inc="act", reads=(bpMM[sp], bsmm),
                               writes=(bprb[sp],))
                        if kc in TRI[s_]:
                            j = TRI[s_].index(kc)
                            p.emit("vector",
                                   lambda E, sp=sp, j=j:
                                   E.tensor_tensor(out=prb[:, sp, :],
                                                   in0=prb[:, sp, :],
                                                   in1=mskv[:, j, :],
                                                   op=ALU.mult),
                                   inc="dve", reads=(bprb[sp], bmsku),
                                   writes=(bprb[sp],))
                        p.emit("tensor",
                               lambda E, kc=kc, sp=sp, cp=cp:
                               E.matmul(pCTX[cp][:], vbuf[:, kc, :],
                                        prb[:, sp, :], start=(kc == 0),
                                        stop=(kc == kcs - 1)),
                               inc="pe" if kc == kcs - 1 else None,
                               reads=(bvbuf, bprb[sp]), writes=(bpCTX[cp],))
                        p.emit("tensor",
                               lambda E, kc=kc, sp=sp, cp=cp:
                               E.matmul(pRS[cp][0:1, :], onb[:, 0:1],
                                        prb[:, sp, :], start=(kc == 0),
                                        stop=(kc == kcs - 1)),
                               inc="pe" if kc == kcs - 1 else None,
                               reads=(bprb[sp], bsmall), writes=(bpRS[cp],))
                    p.emit("vector",
                           lambda E, cp=cp: E.reciprocal(out=stt[0:1, :],
                                                         in_=pRS[cp][0:1, :]),
                           inc="dve", reads=(bpRS[cp],), writes=(bstt,)),
                    p.emit("vector",
                           lambda E: E.tensor_copy(out=rr_b[:],
                                                   in_=stt[0:1, :]),
                           inc="dve", reads=(bstt,), writes=(brr,))
                    p.emit("tensor",
                           lambda E: E.matmul(pBC[:], onb[0:1, :], rr_b[:],
                                              start=True, stop=True),
                           inc="pe", reads=(brr, bsmall), writes=(bpBC,))
                    p.emit("scalar",
                           lambda E: E.activation(out=stt[:], in_=pBC[:],
                                                  func=AF.Copy),
                           inc="act", reads=(bpBC,), writes=(bstt,))
                    p.emit("vector",
                           lambda E, h=h, cp=cp:
                           E.tensor_tensor(out=qcf[:, h, :], in0=pCTX[cp][:],
                                           in1=stt[:], op=ALU.mult),
                           inc="dve", reads=(bpCTX[cp], bstt),
                           writes=(bqcfR.bufs[h],))
            if "ctx" in taps and s_ == 0:
                for h in range(H):
                    p.emit("vector",
                           lambda E, h=h: E.tensor_copy(
                               out=stt[:], in_=qcf[:, h, :]),
                           inc="dve", reads=(bqcfR.bufs[h],), writes=(bstt,))
                    dma(taps["ctx"][:, h, :], stt[:], reads=(bstt,),
                        sem="out")

            def branch_into_x(cname, rhs_buf, rhs_ap_fn, dst_fm_buf,
                              dst_fm_ap_fn, evac_scale, tapn=None):
                def ev(ob, ps, par):
                    dbuf = _rb(dst_fm_buf, ob)
                    if evac_scale is None:
                        p.emit("vector",
                               lambda E, ob=ob, ps=ps:
                               E.tensor_copy(out=dst_fm_ap_fn(ob), in_=ps[:]),
                               inc="dve", reads=(bpMM[par],),
                               writes=(dbuf,))
                    else:
                        p.emit("scalar",
                               lambda E, ob=ob, ps=ps:
                               E.activation(out=dst_fm_ap_fn(ob), in_=ps[:],
                                            func=AF.Copy, scale=evac_scale),
                               inc="act", reads=(bpMM[par], bsmall),
                               writes=(dbuf,))
                    p.emit("tensor",
                           lambda E, ob=ob: E.matmul(
                               pCTX[0][0:1, :], onb[:, 0:1], dst_fm_ap_fn(ob),
                               start=(ob == 0), stop=(ob == NDC - 1)),
                           inc="pe" if ob == NDC - 1 else None,
                           reads=(dbuf, bsmall), writes=(bpCTX[0],))
                    p.emit("scalar",
                           lambda E, ob=ob: E.activation(out=prb[:, 0, :],
                                                         in_=dst_fm_ap_fn(ob),
                                                         func=AF.Square),
                           inc="act", reads=(dbuf,), writes=(bprb[0],))
                    p.emit("tensor",
                           lambda E, ob=ob: E.matmul(
                               pCTX[1][0:1, :], onb[:, 0:1], prb[:, 0, :],
                               start=(ob == 0), stop=(ob == NDC - 1)),
                           inc="pe" if ob == NDC - 1 else None,
                           reads=(bprb[0], bsmall), writes=(bpCTX[1],))

                proj_fm(cname, NDC, rhs_buf, rhs_ap_fn, ev)
                if tapn and tapn in taps and s_ == 0:
                    for dc in range(NDC):
                        p.emit("vector",
                               lambda E, dc=dc: E.tensor_copy(
                                   out=stt[:], in_=dst_fm_ap_fn(dc)),
                               inc="dve", reads=(_rb(dst_fm_buf, dc),),
                               writes=(bstt,))
                        dma(taps[tapn][:, dc, :], stt[:],
                            reads=(bstt,), sem="out")
                p.emit("vector",
                       lambda E: E.tensor_copy(out=srow1[:],
                                               in_=pCTX[0][0:1, :]),
                       inc="dve", reads=(bpCTX[0],), writes=(bsrow,))
                for tc in range(4):
                    p.emit("tensor",
                           lambda E, tc=tc: E.matmul(
                               pBC[:, tc:tc + 1],
                               srow1[0:1, tc * 128:(tc + 1) * 128],
                               onf[0:1, 0:1], start=(tc == 0),
                               stop=(tc == 3)),
                           inc="pe" if tc == 3 else None,
                           reads=(bsrow, bsmall), writes=(bpBC,))
                p.emit("vector",
                       lambda E: E.tensor_copy(out=srow1[:],
                                               in_=pCTX[1][0:1, :]),
                       inc="dve", reads=(bpCTX[1],), writes=(bsrow,))
                for tc in range(4):
                    p.emit("tensor",
                           lambda E, tc=tc: E.matmul(
                               pBC[:, 4 + tc:5 + tc],
                               srow1[0:1, tc * 128:(tc + 1) * 128],
                               onf[0:1, 0:1], start=False,
                               stop=(tc == 3)),
                           inc="pe" if tc == 3 else None,
                           reads=(bsrow, bsmall), writes=(bpBC,))
                p.emit("vector",
                       lambda E: E.tensor_copy(out=stat[:, 0:8],
                                               in_=pBC[:, 0:8]),
                       inc="dve", reads=(bpBC,), writes=(bstat,))
                ln_finalize(4)
                p.emit("vector",
                       lambda E: E.tensor_tensor(out=stat[:, 20:24],
                                                 in0=stat[:, 8:12],
                                                 in1=stat[:, 12:16],
                                                 op=ALU.mult),
                       inc="dve", reads=(bstat,), writes=(bstat,))
                for tc in range(4):
                    sl = tc % 2
                    transpose_fm(
                        lambda dc, tc=tc: dst_fm_ap_fn(dc)[
                            :, tc * 128:(tc + 1) * 128],
                        lambda g4, sl=sl: br_t[:, sl, :].rearrange(
                            "p (a b) -> p a b", a=NDC)[:, 4 * g4:4 * (g4 + 1),
                                                       :],
                        _rball(dst_fm_buf), bbr[sl])
                    p.emit("vector",
                           lambda E, sl=sl, tc=tc:
                           E.tensor_scalar(out=br_t[:, sl, :],
                                           in0=br_t[:, sl, :],
                                           scalar1=stat[:, 12 + tc:13 + tc],
                                           scalar2=stat[:, 20 + tc:21 + tc],
                                           op0=ALU.mult, op1=ALU.subtract),
                           inc="dve", reads=(bbr[sl], bstat),
                           writes=(bbr[sl],))
                    p.emit("vector",
                           lambda E, sl=sl, tc=tc:
                           E.tensor_tensor(out=x_t[:, tc, :],
                                           in0=x_t[:, tc, :],
                                           in1=br_t[:, sl, :], op=ALU.add),
                           inc="dve", reads=(bbr[sl], bx[tc]),
                           writes=(bx[tc],))

            if 'wo' in P2PARTS:
                branch_into_x("wo", bqcfR, lambda dc: qcf[:, dc, :], bxnf,
                              lambda ob: xnf[:, ob, :], None, tapn="ao")
            if "xa" in taps and s_ == 0:
                for tc in range(4):
                    dma(taps["xa"][:, tc, :], x_t[:, tc, :], reads=(bx[tc],),
                        sem="out")

            if 'ffn' not in P2PARTS:
                for tc in range(4):
                    dma(OUT[t0 + tc * 128:t0 + (tc + 1) * 128, :],
                        x_t[:, tc, :], reads=(bx[tc],), sem="out")
                continue
            for tc in range(4):
                xs = x_t[:, tc, :]
                ln_stats(xs, tc, (bx[tc],), qcf_slot(tc), bqcfR.r(8 * tc, 8))
            ln_finalize(4)
            for tc in range(4):
                sl = tc % 2
                p.emit("vector",
                       lambda E, tc=tc, sl=sl:
                       E.tensor_scalar(out=br_t[:, sl, :], in0=x_t[:, tc, :],
                                       scalar1=stat[:, 8 + tc:9 + tc],
                                       scalar2=stat[:, 12 + tc:13 + tc],
                                       op0=ALU.subtract, op1=ALU.mult),
                       inc="dve", reads=(bx[tc], bstat), writes=(bbr[sl],))
                transpose_fm(
                    lambda dc, sl=sl: br_t[:, sl, dc * 128:(dc + 1) * 128],
                    lambda g4, tc=tc: xnf[:, 4 * g4:4 * (g4 + 1),
                                          tc * 128:(tc + 1) * 128],
                    (bbr[sl],), bxnf)

            def relu_evac(svcol, dstbuf, dst_ap_fn):
                def ev(ob, ps, par):
                    p.emit("scalar",
                           lambda E, ob=ob, ps=ps:
                           E.activation(out=dst_ap_fn(ob), in_=ps[:],
                                        func=AF.Relu,
                                        scale=sv[:, svcol:svcol + 1]),
                           inc="act", reads=(bpMM[par], bsmall),
                           writes=(_rb(dstbuf, ob),))
                return ev

            proj_fm("w0", NDC, bxnf, lambda dc: xnf[:, dc, :],
                    relu_evac(0, bqcfR, lambda ob: qcf[:, ob, :]))
            if "h1" in taps and s_ == 0:
                for dc in range(NDC):
                    p.emit("vector",
                           lambda E, dc=dc: E.tensor_copy(
                               out=stt[:], in_=qcf[:, dc, :]),
                           inc="dve", reads=(bqcfR.bufs[dc],),
                           writes=(bstt,))
                    dma(taps["h1"][:, dc, :], stt[:], reads=(bstt,),
                        sem="out")
            proj_fm("w1", NDC, bqcfR, lambda dc: qcf[:, dc, :],
                    relu_evac(1, bxnf, lambda ob: xnf[:, ob, :]))
            if 'f3' in P2PARTS:
                branch_into_x("w2", bxnf, lambda dc: xnf[:, dc, :], bqcfR,
                              lambda ob: qcf[:, ob, :], sv[:, 2:3],
                              tapn="ffo")

            for tc in range(4):
                dma(OUT[t0 + tc * 128:t0 + (tc + 1) * 128, :], x_t[:, tc, :],
                    reads=(bx[tc],), sem="out")

        p.emit("sync", None, deps=[("out", p.cnt.get("out", 0))])

        with nc.Block() as block:
            @block.sync
            def _(E):
                for f in p.q["sync"]:
                    f(E)

            @block.vector
            def _(E):
                for f in p.q["vector"]:
                    f(E)

            @block.scalar
            def _(E):
                for f in p.q["scalar"]:
                    f(E)

            @block.tensor
            def _(E):
                for f in p.q["tensor"]:
                    f(E)

            @block.gpsimd
            def _(E):
                for f in p.q["gpsimd"]:
                    f(E)

    return nc


def _host_tables():
    inv_freq = (1.0 / (10000.0 ** (np.arange(0, 128, 2, dtype=np.float32)
                                   / np.float32(128)))).astype(np.float32)
    ang = np.arange(S, dtype=np.float32)[:, None] * inv_freq[None, :]
    cos_h = np.cos(ang).astype(np.float32)
    sin_h = np.sin(ang).astype(np.float32)
    cosd = np.empty((128, S), np.float32)
    sins = np.empty((128, S), np.float32)
    cosd[0:64] = cos_h.T
    cosd[64:128] = cos_h.T
    sins[0:64] = sin_h.T
    sins[64:128] = sin_h.T
    return cosd.astype(ml_dtypes.bfloat16), sins.astype(ml_dtypes.bfloat16)


def _host_masks(half):
    q0s = Q0S[half]
    cm = np.zeros((128, NT, 32), np.float32)
    msk = np.zeros((NT, 128, 8, T), np.uint8)
    for s_ in range(NT):
        q0 = q0s[s_]
        qpos = q0 + np.arange(T)
        for kc in range(KCS[s_]):
            if kc in TRI[s_]:
                j = TRI[s_].index(kc)
                kpos = kc * 128 + np.arange(128)
                msk[s_, :, j, :] = (
                    kpos[:, None] <= qpos[None, :]).astype(np.uint8)
            elif kc > q0 // 128 + 3:
                cm[:, s_, kc] = NEG
    return cm, msk


def _prep(L0_attn_x, L0_attn_Wq, L0_attn_Wk, L0_attn_Wv, L0_attn_Wo,
          L0_ffn_k0, L0_ffn_s0, L0_ffn_k1, L0_ffn_s1, L0_ffn_k2, L0_ffn_s2):
    x = np.ascontiguousarray(np.asarray(L0_attn_x, dtype=np.float32))
    wq = np.ascontiguousarray(np.asarray(L0_attn_Wq, dtype=np.float32))
    wk = np.ascontiguousarray(np.asarray(L0_attn_Wk, dtype=np.float32))
    wv = np.ascontiguousarray(np.asarray(L0_attn_Wv, dtype=np.float32))
    wo = np.ascontiguousarray(np.asarray(L0_attn_Wo, dtype=np.float32))
    k0 = np.ascontiguousarray(np.asarray(L0_ffn_k0)).view(np.uint8)
    k1 = np.ascontiguousarray(np.asarray(L0_ffn_k1)).view(np.uint8)
    k2 = np.ascontiguousarray(np.asarray(L0_ffn_k2)).view(np.uint8)
    svin = np.stack([np.float32(L0_ffn_s0), np.float32(L0_ffn_s1),
                     np.float32(L0_ffn_s2)]).reshape(1, 3)

    cosd, sins = _host_tables()
    cv1 = np.ones((128, 32), np.float32)
    for dc in range(32):
        idx = dc * 128 + np.arange(128)
        cv1[:, dc] = np.where(idx < NE, 1.0, -4.0)
    cv2 = (2.0 * cv1).astype(np.float32)
    idn = np.eye(128, dtype=np.float32).astype(ml_dtypes.bfloat16)
    rmat = np.zeros((128, 128), np.float32)
    for i in range(64):
        rmat[i, i + 64] = -1.0
        rmat[i + 64, i] = 1.0
    rott = np.ascontiguousarray(rmat.T)

    in_maps = []
    for c in range(8):
        b, half = c // 2, c % 2
        q0s = Q0S[half]
        xown = np.ascontiguousarray(
            np.concatenate([x[b, q0:q0 + T] for q0 in q0s], axis=0))
        pos = np.concatenate([q0 + np.arange(T) for q0 in q0s])
        cosq = np.ascontiguousarray(cosd[:, pos])
        sinq = np.ascontiguousarray(sins[:, pos])
        cm, msk = _host_masks(half)
        in_maps.append({
            "xown": xown, "xkv": np.ascontiguousarray(x[b]),
            "wq": wq, "wk": wk, "wv": wv, "wo": wo,
            "k0": k0, "k1": k1, "k2": k2, "svin": svin,
            "cosk": cosd, "sink": sins, "cosq": cosq, "sinq": sinq,
            "cmv": cm, "msk01": msk, "cv1": cv1, "cv2": cv2, "idn": idn,
            "rot": rott,
        })

    return in_maps


def kernel(**inputs):
    in_maps = _prep(**inputs)
    nc = build_program(DEBUG_TAPS)
    res = run_bass_kernel_spmd(nc, in_maps, core_ids=list(range(8)))

    out = np.empty((B, S, D), np.float32)
    for c in range(8):
        b, half = c // 2, c % 2
        o = res.results[c]["out"]
        for s_, q0 in enumerate(Q0S[half]):
            out[b, q0:q0 + T] = o[s_ * T:(s_ + 1) * T]
    return out


if __name__ == "__main__":
    rng = np.random.default_rng(0)
    print("building program...")
    nc = build_program()
    n = nc.n_instructions() if hasattr(nc, "n_instructions") else "?"
    print("built, instructions:", n)
